# version 02 (final)
# baseline (speedup 1.0000x reference)
import numpy as np
import ml_dtypes
from contextlib import ExitStack

import concourse.bass as bass
import concourse.mybir as mybir
from concourse.bass_utils import run_bass_kernel_spmd

F32 = mybir.dt.float32
BF16 = mybir.dt.bfloat16
AF = mybir.ActivationFunctionType
ALU = mybir.AluOpType
AX = mybir.AxisListType

NCORE = 8
NSEQ = 2
S = 2048
D = 1024
H = 16
NE = 8
DI = 2048
T = NSEQ * S
NT128 = T // 128
TCH = 1024
NTCH = T // TCH
QB = 512
NQB = S // QB
NKC = S // 128
VT_CW = H * 65
LNEPS = 1e-5
TKEPS = 1e-5
INV_D = 1.0 / D

PHASE = "full"
SKIP_W = False
A_MAX_PR = 8
A_DO_UNITS = True
A_DO_FC = True
A_SIMPLE_QKEV = False
A_SKIP_PROJ = False
A_PAIR_PARTS = 7
DRAIN_PE = False
DRAIN_PE_VAL = None

_WRAP = {"tensor_tensor", "tensor_scalar", "tensor_reduce", "tensor_copy",
         "reciprocal", "memset", "activation", "copy", "add", "mul",
         "transpose", "select", "iota", "tensor_tensor_scan"}


class Sem:
    def __init__(self, h):
        self.h = h
        self.n = 0

    def inc(self, amt=1):
        self.n += amt
        return self.n


class _DummyInst:
    __slots__ = ("entry",)

    def __init__(self, entry):
        self.entry = entry

    def then_inc(self, h, amt):
        self.entry["incs"].append((h, amt))
        assert len(self.entry["incs"]) <= 1, "one sync update per instruction"
        return self


class Rec:

    def __init__(self, name, fence=None):
        self.name = name
        self.fence = fence
        self.ops = []

    def _push(self, m, a, k):
        entry = {"m": m, "a": a, "k": k, "incs": []}
        self.ops.append(entry)
        return _DummyInst(entry)

    def __getattr__(self, name):
        if name.startswith("_"):
            raise AttributeError(name)
        def call(*a, **k):
            if self.fence is not None and name in _WRAP:
                if self.fence.n:
                    self._push("wait_ge", (self.fence.h, self.fence.n), {})
                inst = self._push(name, a, k)
                inst.entry["incs"].append((self.fence.h, 1))
                self.fence.inc()
                return _DummyInst({"incs": []})
            return self._push(name, a, k)
        return call


def build_program(phase=None):
    if phase is None:
        phase = PHASE
    nc = bass.Bass("TRN2", target_bir_lowering=False, debug=False,
                   detect_race_conditions=False)

    ctx = ExitStack()
    e = ctx.enter_context

    d_x = Sem(e(nc.semaphore("d_x")))
    d_xn = Sem(e(nc.semaphore("d_xn")))
    d_w = Sem(e(nc.semaphore("d_w")))
    d_a = Sem(e(nc.semaphore("d_a")))
    d_s = Sem(e(nc.semaphore("d_s")))
    d_sm = Sem(e(nc.semaphore("d_sm")))
    d_cast = Sem(e(nc.semaphore("d_cast")))
    t_xn = Sem(e(nc.semaphore("t_xn")))
    t_aux = Sem(e(nc.semaphore("t_aux")))
    t_s = Sem(e(nc.semaphore("t_s")))
    t_wt = Sem(e(nc.semaphore("t_wt")))
    t_out = Sem(e(nc.semaphore("t_out")))
    br = [Sem(e(nc.semaphore(f"br{i}"))) for i in range(8)]
    fence_v = Sem(e(nc.semaphore("fence_v")))
    fence_s = Sem(e(nc.semaphore("fence_s")))
    fence_g = Sem(e(nc.semaphore("fence_g")))

    REC = {
        "sync": Rec("sync"),
        "tensor": Rec("tensor"),
        "vector": Rec("vector", fence_v),
        "scalar": Rec("scalar", fence_s),
        "gpsimd": Rec("gpsimd", fence_g),
    }
    FOF = {"vector": fence_v, "scalar": fence_s, "gpsimd": fence_g}
    SY, TE, VE, SC, GP = (REC[k] for k in ("sync", "tensor", "vector", "scalar", "gpsimd"))

    def tok(eng):
        f = FOF[eng]
        return (f, f.n)

    pb = [e(nc.psum_tensor(f"pb{i}", [128, 512], F32)) for i in range(8)]
    bank_free = [None] * 8

    def wait_toks(q, waits):
        for wv in waits:
            if wv is None:
                continue
            sem, val = wv
            if val:
                q.wait_ge(sem.h, val)

    def pe_group(b, body, extra_waits=()):
        t = REC["tensor"]
        wait_toks(t, extra_waits)
        if bank_free[b] is not None and bank_free[b][1]:
            t.wait_ge(bank_free[b][0].h, bank_free[b][1])
        body(t).then_inc(br[b].h, 1)
        return br[b].inc()

    def evac(b, eng, body, extra_waits=()):
        q = REC[eng]
        wait_toks(q, extra_waits)
        q.wait_ge(br[b].h, br[b].n)
        body(q)
        tk = tok(eng)
        bank_free[b] = tk
        return tk

    def pe_tokens():
        return tuple((br[i], br[i].n) for i in range(8))

    x_in = nc.dram_tensor("x", [NSEQ, S, D], F32, kind="ExternalInput").ap()
    hm_in = nc.dram_tensor("head_mask", [H], F32, kind="ExternalInput").ap()
    qkvw_in = nc.dram_tensor("qkv_w", [3 * D, D], F32, kind="ExternalInput").ap()
    qkvb_in = nc.dram_tensor("qkv_b", [3 * D], F32, kind="ExternalInput").ap()
    fcw_in = nc.dram_tensor("fc_w", [D, D], F32, kind="ExternalInput").ap()
    fcb_in = nc.dram_tensor("fc_b", [D], F32, kind="ExternalInput").ap()
    lns_in = nc.dram_tensor("ln_scale", [D], F32, kind="ExternalInput").ap()
    lnb_in = nc.dram_tensor("ln_bias", [D], F32, kind="ExternalInput").ap()
    gw_in = nc.dram_tensor("gate_w", [NE, D], F32, kind="ExternalInput").ap()
    Wg_in = nc.dram_tensor("Wg", [NE, D, DI], F32, kind="ExternalInput").ap()
    Wu_in = nc.dram_tensor("Wu", [NE, D, DI], F32, kind="ExternalInput").ap()
    Wd_in = nc.dram_tensor("Wd", [NE, DI, D], F32, kind="ExternalInput").ap()
    sWg_in = nc.dram_tensor("sWg", [D, DI], F32, kind="ExternalInput").ap()
    sWu_in = nc.dram_tensor("sWu", [D, DI], F32, kind="ExternalInput").ap()
    sWd_in = nc.dram_tensor("sWd", [DI, D], F32, kind="ExternalInput").ap()
    out_hbm = nc.dram_tensor("out", [NSEQ, S, D], F32, kind="ExternalOutput").ap()

    scr_xn = nc.dram_tensor("scr_xn", [8, 128, T], BF16, kind="Internal").ap()
    scr_s = nc.dram_tensor("scr_s", [T, D], F32, kind="Internal").ap()
    scr_wfm = nc.dram_tensor("scr_wfm", [NE, T], F32, kind="Internal").ap()
    scr_c10 = nc.dram_tensor("scr_c10", [2, NE], F32, kind="Internal").ap()
    wg_bf = nc.dram_tensor("wg_bf", [9, D, DI], BF16, kind="Internal").ap()
    wu_bf = nc.dram_tensor("wu_bf", [9, D, DI], BF16, kind="Internal").ap()
    wd_bf = nc.dram_tensor("wd_bf", [9, DI, D], BF16, kind="Internal").ap()
    qkvw_bf = nc.dram_tensor("qkvw_bf", [3 * D, D], BF16, kind="Internal").ap()
    fcw_bf = nc.dram_tensor("fcw_bf", [D, D], BF16, kind="Internal").ap()
    scr_wqkT = nc.dram_tensor("scr_wqkT", [8, 8, 128, 256], BF16, kind="Internal").ap()
    scr_wvT = nc.dram_tensor("scr_wvT", [8, 128, D], BF16, kind="Internal").ap()
    scr_fcT = nc.dram_tensor("scr_fcT", [8, 128, D], BF16, kind="Internal").ap()

    dbg = {}
    if phase in ("p0", "p0a"):
        dbg["dbg_xn"] = nc.dram_tensor("dbg_xn", [8, 128, T], BF16, kind="ExternalOutput").ap()
        dbg["dbg_wfm"] = nc.dram_tensor("dbg_wfm", [NE, T], F32, kind="ExternalOutput").ap()
    if phase == "p0a":
        dbg["dbg_s"] = nc.dram_tensor("dbg_s", [T, D], F32, kind="ExternalOutput").ap()

    ident = e(nc.sbuf_tensor("ident", [128, 128], F32))
    ones_row = e(nc.sbuf_tensor("ones_row", [1, 128], F32))
    ones_col = e(nc.sbuf_tensor("ones_col", [128, 1], F32))
    hmcol = e(nc.sbuf_tensor("hmcol", [128, H], F32))
    hm_fb = e(nc.sbuf_tensor("hm_fb", [128, D], BF16))
    bqkhm = e(nc.sbuf_tensor("bqkhm", [128, H], F32))
    bvhm_b = e(nc.sbuf_tensor("bvhm_b", [128, D], F32))
    fcb_b = e(nc.sbuf_tensor("fcb_b", [128, D], F32))

    p0_ctx = ExitStack()
    ep = p0_ctx.enter_context
    iot = ep(nc.sbuf_tensor("iot", [128, 128], F32))
    onesb64 = ep(nc.sbuf_tensor("onesb64", [128, 64], F32))
    qkvbqk_nat = ep(nc.sbuf_tensor("qkvbqk_nat", [H, 128], F32))
    lnsnat = ep(nc.sbuf_tensor("lnsnat", [8, 128], F32))
    lnbnat = ep(nc.sbuf_tensor("lnbnat", [8, 128], F32))
    lnsT = ep(nc.sbuf_tensor("lnsT", [128, 8], F32))
    lnbT = ep(nc.sbuf_tensor("lnbT", [128, 8], F32))
    gw_nat = ep(nc.sbuf_tensor("gw_nat", [NE, D], F32))
    gwT = ep(nc.sbuf_tensor("gwT", [128, 64], F32))
    gw2T = ep(nc.sbuf_tensor("gw2T", [128, 64], F32))
    gwbT = ep(nc.sbuf_tensor("gwbT", [128, 64], F32))
    c10row = ep(nc.sbuf_tensor("c10row", [1, 2 * NE], F32))
    c1b = ep(nc.sbuf_tensor("c1b", [128, NE], F32))
    c0b = ep(nc.sbuf_tensor("c0b", [128, NE], F32))
    wtr = [ep(nc.sbuf_tensor(f"wtr{i}", [128, 3 * D], BF16)) for i in range(2)]

    GP.iota(out=iot[:], pattern=[[-1, 128]], base=0, channel_multiplier=1,
            allow_small_or_imprecise_dtypes=True)
    iota_tok = tok("gpsimd")

    def cast(dst, src):
        GP.dma_start(out=dst, in_=src).then_inc(d_cast.h, 16)
        return d_cast.inc(16)

    if not SKIP_W:
        cast(qkvw_bf[:], qkvw_in[:])
        c_fc = cast(fcw_bf[:], fcw_in[:])
        cast(wg_bf[0], sWg_in[:])
        cast(wu_bf[0], sWu_in[:])
        cast(wd_bf[0], sWd_in[:])
        for ex in range(NE):
            cast(wg_bf[1 + ex], Wg_in[ex])
            cast(wu_bf[1 + ex], Wu_in[ex])
            cast(wd_bf[1 + ex], Wd_in[ex])
        casts_all = d_cast.n
    else:
        c_fc = 0
        casts_all = 0

    def ld_small(dst, src):
        SY.dma_start(out=dst, in_=src).then_inc(d_sm.h, 16)
        return d_sm.inc(16)

    ld_small(qkvbqk_nat[:], qkvb_in.rearrange("(h x) -> h x", x=192)[:, 0:128])
    ld_small(lnsnat[:], lns_in.rearrange("(c x) -> c x", x=128))
    ld_small(lnbnat[:], lnb_in.rearrange("(c x) -> c x", x=128))
    ld_small(gw_nat[:], gw_in[:])
    ld_small(hmcol[:], hm_in.unsqueeze(0).partition_broadcast(128).squeeze(1))
    ld_small(fcb_b[:], fcb_in.unsqueeze(0).partition_broadcast(128).squeeze(1))
    sm_all = ld_small(bvhm_b[:].rearrange("p (h x) -> p h x", x=64),
                      qkvb_in.rearrange("(h x) -> h x", x=192)[:, 128:192]
                      .unsqueeze(0).partition_broadcast(128).squeeze(1))

    VE.wait_ge(fence_g.h, iota_tok[1])
    VE.wait_ge(d_sm.h, sm_all)
    VE.tensor_scalar(out=ident[:], in0=iot[:], scalar1=0.0, scalar2=None, op0=ALU.is_equal)
    VE.memset(ones_row[:], 1.0)
    VE.memset(ones_col[:], 1.0)
    VE.memset(onesb64[:], 1.0)
    for h in range(H):
        VE.tensor_scalar(out=hm_fb[:, h * 64:(h + 1) * 64], in0=onesb64[:],
                         scalar1=hmcol[:, h:h + 1], scalar2=None, op0=ALU.mult)
        VE.tensor_scalar(out=bvhm_b[:, h * 64:(h + 1) * 64], in0=bvhm_b[:, h * 64:(h + 1) * 64],
                         scalar1=hmcol[:, h:h + 1], scalar2=None, op0=ALU.mult)
    v_ident = tok("vector")

    pe_group(5, lambda t: t.transpose(out=pb[5][0:128, 0:16], in_=qkvbqk_nat[:],
                                      identity=ident[0:16, 0:16]),
             extra_waits=[v_ident])
    evac(5, "vector", lambda v: v.tensor_copy(out=bqkhm[:, 0:16], in_=pb[5][0:128, 0:16]))
    pe_group(5, lambda t: t.transpose(out=pb[5][0:128, 0:8], in_=lnsnat[:], identity=ident[0:8, 0:8]))
    evac(5, "vector", lambda v: v.tensor_copy(out=lnsT[:], in_=pb[5][0:128, 0:8]))
    pe_group(5, lambda t: t.transpose(out=pb[5][0:128, 0:8], in_=lnbnat[:], identity=ident[0:8, 0:8]))
    evac(5, "vector", lambda v: v.tensor_copy(out=lnbT[:], in_=pb[5][0:128, 0:8]))
    for c in range(8):
        pe_group(5, lambda t, c=c: t.transpose(out=pb[5][0:128, 0:8],
                                               in_=gw_nat[:, c * 128:(c + 1) * 128],
                                               identity=ident[0:8, 0:8]))
        evac(5, "vector", lambda v, c=c: v.tensor_copy(out=gwT[:, c * 8:(c + 1) * 8],
                                                       in_=pb[5][0:128, 0:8]))

    VE.tensor_tensor(out=bqkhm[:], in0=bqkhm[:], in1=hmcol[:], op=ALU.mult)
    for c in range(8):
        VE.tensor_scalar(out=gw2T[:, c * 8:(c + 1) * 8], in0=gwT[:, c * 8:(c + 1) * 8],
                         scalar1=lnsT[:, c:c + 1], scalar2=None, op0=ALU.mult)
        VE.tensor_scalar(out=gwbT[:, c * 8:(c + 1) * 8], in0=gwT[:, c * 8:(c + 1) * 8],
                         scalar1=lnbT[:, c:c + 1], scalar2=None, op0=ALU.mult)
    v_pre2_done = tok("vector")

    def c10_body(t):
        for c in range(8):
            t.matmul(out=pb[6][0:1, 0:8], lhsT=ones_col[:], rhs=gw2T[:, c * 8:(c + 1) * 8],
                     start=(c == 0), stop=(c == 7))
        for c in range(8):
            last = t.matmul(out=pb[6][0:1, 8:16], lhsT=ones_col[:], rhs=gwbT[:, c * 8:(c + 1) * 8],
                            start=(c == 0), stop=(c == 7))
        return last
    pe_group(6, c10_body, extra_waits=[v_pre2_done])
    c10_tok = evac(6, "vector", lambda v: v.tensor_copy(out=c10row[0:1, 0:16], in_=pb[6][0:1, 0:16]))
    SY.wait_ge(fence_v.h, c10_tok[1])
    SY.dma_start(out=scr_c10.rearrange("a b -> (a b)").unsqueeze(0), in_=c10row[:]).then_inc(t_aux.h, 16)
    tgt_c10st = t_aux.inc(16)
    SY.wait_ge(t_aux.h, tgt_c10st)
    SY.dma_start(out=c1b[:], in_=scr_c10[0].unsqueeze(0).partition_broadcast(128).squeeze(1)).then_inc(d_sm.h, 16)
    d_sm.inc(16)
    SY.dma_start(out=c0b[:], in_=scr_c10[1].unsqueeze(0).partition_broadcast(128).squeeze(1)).then_inc(d_sm.h, 16)
    prelude_bcast_done = d_sm.inc(16)

    if not SKIP_W:
        wt_store_done = []
        fc_store_done = []
        for dch in range(8):
            buf = wtr[dch % 2]
            if dch == 0:
                SY.wait_ge(d_cast.h, c_fc)
            if dch >= 2:
                SY.wait_ge(t_wt.h, wt_store_done[dch - 2])
            SY.dma_start(out=buf[:], in_=qkvw_bf[:, dch * 128:(dch + 1) * 128],
                         transpose=True).then_inc(d_a.h, 16)
            tgt_tr = d_a.inc(16)
            SY.wait_ge(d_a.h, tgt_tr)
            for pr in range(8):
                src = buf[:].rearrange("p (h x) -> p h x", x=192)[:, 2 * pr:2 * pr + 2, 0:128]
                SY.dma_start(out=scr_wqkT[pr, dch].rearrange("p (h x) -> p h x", x=128),
                             in_=src).then_inc(t_wt.h, 16)
                t_wt.inc(16)
            srcv = buf[:].rearrange("p (h x) -> p h x", x=192)[:, :, 128:192]
            SY.dma_start(out=scr_wvT[dch].rearrange("p (h x) -> p h x", x=64),
                         in_=srcv).then_inc(t_wt.h, 16)
            t_wt.inc(16)
            wt_store_done.append(t_wt.n)
        for dch in range(8):
            buf = wtr[dch % 2]
            w_prev = wt_store_done[6 + dch] if dch < 2 else fc_store_done[dch - 2]
            SY.wait_ge(t_wt.h, w_prev)
            SY.dma_start(out=buf[:, 0:D], in_=fcw_bf[:, dch * 128:(dch + 1) * 128],
                         transpose=True).then_inc(d_a.h, 16)
            tgt_tr = d_a.inc(16)
            SY.wait_ge(d_a.h, tgt_tr)
            SY.dma_start(out=scr_fcT[dch], in_=buf[:, 0:D]).then_inc(t_wt.h, 16)
            fc_store_done.append(t_wt.inc(16))
        wt_all = t_wt.n
    else:
        wt_all = 0

    x_sb = [ep(nc.sbuf_tensor(f"x_sb{i}", [128, D], F32)) for i in range(2)]
    junk_sb = ep(nc.sbuf_tensor("junk_sb", [128, D], F32))
    xn_tm = ep(nc.sbuf_tensor("xn_tm", [128, D], F32))
    xnf32 = ep(nc.sbuf_tensor("xnf32", [128, D], F32))
    xnbf_acc = ep(nc.sbuf_tensor("xnbf_acc", [128, 8 * 512], BF16))
    acc_sx = ep(nc.sbuf_tensor("acc_sx", [128, 1], F32))
    acc_sxx = ep(nc.sbuf_tensor("acc_sxx", [128, 1], F32))
    acc_sn = ep(nc.sbuf_tensor("acc_sn", [128, 1], F32))
    acc_snn = ep(nc.sbuf_tensor("acc_snn", [128, 1], F32))
    mu_c = ep(nc.sbuf_tensor("mu_c", [128, 1], F32))
    var_c = ep(nc.sbuf_tensor("var_c", [128, 1], F32))
    rstd_c = ep(nc.sbuf_tensor("rstd_c", [128, 1], F32))
    nmrs_c = ep(nc.sbuf_tensor("nmrs_c", [128, 1], F32))
    tmp_c = ep(nc.sbuf_tensor("tmp_c", [128, 1], F32))
    tmp2_c = ep(nc.sbuf_tensor("tmp2_c", [128, 1], F32))
    mu2_c = ep(nc.sbuf_tensor("mu2_c", [128, 1], F32))
    var2_c = ep(nc.sbuf_tensor("var2_c", [128, 1], F32))
    r2_c = ep(nc.sbuf_tensor("r2_c", [128, 1], F32))
    lg_a = ep(nc.sbuf_tensor("lg_a", [128, NE], F32))
    lg_b = ep(nc.sbuf_tensor("lg_b", [128, NE], F32))
    e8_c = ep(nc.sbuf_tensor("e8_c", [128, NE], F32))
    p8_c = ep(nc.sbuf_tensor("p8_c", [128, NE], F32))
    ge1_c = ep(nc.sbuf_tensor("ge1_c", [128, NE], F32))
    lgm_c = ep(nc.sbuf_tensor("lgm_c", [128, NE], F32))
    pm_c = ep(nc.sbuf_tensor("pm_c", [128, NE], F32))
    m1_c = ep(nc.sbuf_tensor("m1_c", [128, 1], F32))
    m1n_c = ep(nc.sbuf_tensor("m1n_c", [128, 1], F32))
    m2_c = ep(nc.sbuf_tensor("m2_c", [128, 1], F32))
    d8_c = ep(nc.sbuf_tensor("d8_c", [128, 1], F32))
    s2_c = ep(nc.sbuf_tensor("s2_c", [128, 1], F32))
    rw_c = ep(nc.sbuf_tensor("rw_c", [128, 1], F32))
    wtm_c = ep(nc.sbuf_tensor("wtm_c", [128, NE], F32))
    wfm_sb = ep(nc.sbuf_tensor("wfm_sb", [NE, T], F32))

    sc_xn_done = [None] * NT128
    xn_store_tgt = {}
    pe_T1_done = [None] * NT128
    pe_T2_done = [None] * NT128

    for i in range(NT128):
        seq, t0 = divmod(i * 128, S)
        if i >= 2:
            SY.wait_ge(fence_s.h, sc_xn_done[i - 2][1])
        SY.dma_start(out=x_sb[i % 2][:], in_=x_in[seq, t0:t0 + 128, :]).then_inc(d_x.h, 16)
        tgt_x = d_x.inc(16)

        if i == 0:
            SC.wait_ge(fence_v.h, v_pre2_done[1])
        SC.wait_ge(d_x.h, tgt_x)
        xb = x_sb[i % 2][:]
        SC.activation(out=junk_sb[:], in_=xb, func=AF.Identity, accum_out=acc_sx[:])
        SC.activation(out=junk_sb[:], in_=xb, func=AF.Square, accum_out=acc_sxx[:])
        sA = tok("scalar")

        VE.wait_ge(fence_s.h, sA[1])
        VE.tensor_scalar(out=mu_c[:], in0=acc_sx[:], scalar1=INV_D, scalar2=None, op0=ALU.mult)
        VE.tensor_scalar(out=var_c[:], in0=acc_sxx[:], scalar1=INV_D, scalar2=None, op0=ALU.mult)
        VE.tensor_tensor(out=tmp_c[:], in0=mu_c[:], in1=mu_c[:], op=ALU.mult)
        VE.tensor_tensor(out=var_c[:], in0=var_c[:], in1=tmp_c[:], op=ALU.subtract)
        VE.tensor_scalar(out=var_c[:], in0=var_c[:], scalar1=LNEPS, scalar2=None, op0=ALU.add)
        vA = tok("vector")

        SC.wait_ge(fence_v.h, vA[1])
        SC.activation(out=tmp2_c[:], in_=var_c[:], func=AF.Ln)
        SC.activation(out=rstd_c[:], in_=tmp2_c[:], func=AF.Exp, scale=-0.5)
        sB = tok("scalar")

        VE.wait_ge(fence_s.h, sB[1])
        VE.tensor_tensor(out=nmrs_c[:], in0=mu_c[:], in1=rstd_c[:], op=ALU.mult)
        VE.tensor_scalar(out=nmrs_c[:], in0=nmrs_c[:], scalar1=-1.0, scalar2=None, op0=ALU.mult)
        vB = tok("vector")

        SC.wait_ge(fence_v.h, vB[1])
        if i > 0:
            wait_toks(SC, pe_T1_done[i - 1])
        SC.activation(out=xn_tm[:], in_=xb, func=AF.Identity, scale=rstd_c[:], bias=nmrs_c[:],
                      accum_out=acc_sn[:])
        SC.activation(out=junk_sb[:], in_=xn_tm[:], func=AF.Square, accum_out=acc_snn[:])
        sC = tok("scalar")
        sc_xn_done[i] = sC

        VE.wait_ge(fence_s.h, sC[1])
        VE.tensor_scalar(out=mu2_c[:], in0=acc_sn[:], scalar1=INV_D, scalar2=None, op0=ALU.mult)
        VE.tensor_scalar(out=var2_c[:], in0=acc_snn[:], scalar1=INV_D, scalar2=None, op0=ALU.mult)
        VE.tensor_tensor(out=tmp_c[:], in0=mu2_c[:], in1=mu2_c[:], op=ALU.mult)
        VE.tensor_tensor(out=var2_c[:], in0=var2_c[:], in1=tmp_c[:], op=ALU.subtract)
        VE.tensor_scalar(out=var2_c[:], in0=var2_c[:], scalar1=LNEPS, scalar2=None, op0=ALU.add)
        vC = tok("vector")

        SC.wait_ge(fence_v.h, vC[1])
        SC.activation(out=tmp2_c[:], in_=var2_c[:], func=AF.Ln)
        SC.activation(out=r2_c[:], in_=tmp2_c[:], func=AF.Exp, scale=-0.5)
        sD = tok("scalar")

        for c in range(8):
            b = c % 4
            pe_group(b, lambda t, c=c, b=b: t.transpose(
                out=pb[b][0:128, 0:128], in_=xn_tm[:, c * 128:(c + 1) * 128], identity=ident[:]),
                extra_waits=[(fence_s, sC[1])] if c == 0 else [])
            ew_e = []
            if c == 0 and i > 0:
                ew_e = list(pe_T2_done[i - 1])
            evac(b, "vector", lambda v, c=c, b=b: v.tensor_copy(
                out=xnf32[:, c * 128:(c + 1) * 128], in_=pb[b][0:128, 0:128]),
                extra_waits=ew_e)
        pe_T1_done[i] = pe_tokens()
        if i % 4 == 0 and i > 0:
            VE.wait_ge(t_xn.h, xn_store_tgt[i // 4 - 1])
        j = i % 4
        for c in range(8):
            VE.tensor_copy(out=xnbf_acc[:, c * 512 + j * 128: c * 512 + (j + 1) * 128],
                           in_=xnf32[:, c * 128:(c + 1) * 128])
        vD = tok("vector")

        def t2(t):
            for c in range(8):
                last = t.matmul(out=pb[4][0:128, 0:8], lhsT=xnf32[:, c * 128:(c + 1) * 128],
                                rhs=gw2T[:, c * 8:(c + 1) * 8], start=(c == 0), stop=(c == 7))
            return last
        ew = [(fence_v, vD[1])]
        if i == 0:
            ew.append((d_sm, prelude_bcast_done))
        pe_group(4, t2, extra_waits=ew)
        pe_T2_done[i] = pe_tokens()

        VE.wait_ge(fence_s.h, sD[1])
        VE.tensor_tensor(out=tmp_c[:], in0=r2_c[:], in1=mu2_c[:], op=ALU.mult)
        VE.tensor_scalar(out=lg_b[:], in0=c1b[:], scalar1=tmp_c[:], scalar2=None, op0=ALU.mult)
        evac(4, "vector", lambda v: v.tensor_scalar(out=lg_a[:], in0=pb[4][0:128, 0:8],
                                                    scalar1=r2_c[:], scalar2=None, op0=ALU.mult))
        VE.tensor_tensor(out=lg_a[:], in0=lg_a[:], in1=lg_b[:], op=ALU.subtract)
        VE.tensor_tensor(out=lg_a[:], in0=lg_a[:], in1=c0b[:], op=ALU.add)
        VE.tensor_reduce(out=m1_c[:], in_=lg_a[:], axis=AX.X, op=ALU.max)
        VE.tensor_scalar(out=m1n_c[:], in0=m1_c[:], scalar1=-1.0, scalar2=None, op0=ALU.mult)
        vE = tok("vector")

        SC.wait_ge(fence_v.h, vE[1])
        SC.activation(out=e8_c[:], in_=lg_a[:], func=AF.Exp, bias=m1n_c[:], accum_out=d8_c[:])
        sE = tok("scalar")

        VE.wait_ge(fence_s.h, sE[1])
        VE.reciprocal(out=rw_c[:], in_=d8_c[:])
        VE.tensor_scalar(out=p8_c[:], in0=e8_c[:], scalar1=rw_c[:], scalar2=None, op0=ALU.mult)
        VE.tensor_scalar(out=ge1_c[:], in0=lg_a[:], scalar1=m1_c[:], scalar2=None, op0=ALU.is_ge)
        VE.tensor_scalar(out=ge1_c[:], in0=ge1_c[:], scalar1=-1e30, scalar2=None, op0=ALU.mult)
        VE.tensor_tensor(out=lgm_c[:], in0=lg_a[:], in1=ge1_c[:], op=ALU.add)
        VE.tensor_reduce(out=m2_c[:], in_=lgm_c[:], axis=AX.X, op=ALU.max)
        VE.tensor_scalar(out=pm_c[:], in0=lg_a[:], scalar1=m2_c[:], scalar2=None, op0=ALU.is_ge)
        VE.tensor_tensor(out=pm_c[:], in0=pm_c[:], in1=p8_c[:], op=ALU.mult)
        VE.tensor_reduce(out=s2_c[:], in_=pm_c[:], axis=AX.X, op=ALU.add)
        VE.tensor_scalar(out=s2_c[:], in0=s2_c[:], scalar1=TKEPS, scalar2=None, op0=ALU.add)
        VE.reciprocal(out=rw_c[:], in_=s2_c[:])
        VE.tensor_scalar(out=wtm_c[:], in0=pm_c[:], scalar1=rw_c[:], scalar2=None, op0=ALU.mult)
        vF = tok("vector")

        pe_group(7, lambda t: t.transpose(out=pb[7][0:8, 0:128], in_=wtm_c[:], identity=ident[:]),
                 extra_waits=[(fence_v, vF[1])])
        evac(7, "vector", lambda v, i=i: v.tensor_copy(out=wfm_sb[:, i * 128:(i + 1) * 128],
                                                       in_=pb[7][0:8, 0:128]))

        if i % 4 == 3:
            g = i // 4
            SY.wait_ge(fence_v.h, vD[1])
            for c in range(8):
                SY.dma_start(out=scr_xn[c][:, g * 512:(g + 1) * 512],
                             in_=xnbf_acc[:, c * 512:(c + 1) * 512]).then_inc(t_xn.h, 16)
                t_xn.inc(16)
            xn_store_tgt[g] = t_xn.n
    t_xn_all = t_xn.n
    v_wfm = tok("vector")

    SY.wait_ge(fence_v.h, v_wfm[1])
    SY.dma_start(out=scr_wfm[:], in_=wfm_sb[:]).then_inc(t_aux.h, 16)
    tgt_wfm = t_aux.inc(16)
    p0_done = dict(s=tok("scalar"), v=tok("vector"), pe=pe_tokens())
    p0_ctx.close()

    emit_attention = phase in ("p0a", "full")
    t_s_all = 0
    a_done = None

    if emit_attention:
        a_ctx = ExitStack()
        ea = a_ctx.enter_context
        wqk_sb = [ea(nc.sbuf_tensor(f"wqk_sb{i}", [128, 8 * 256], BF16)) for i in range(2)]
        wvt_w = ea(nc.sbuf_tensor("wvt_w", [128, 8 * D], BF16))
        fct_w = ea(nc.sbuf_tensor("fct_w", [128, 8 * D], BF16))
        xn_a = [ea(nc.sbuf_tensor(f"xn_a{i}", [128, 8 * 512], BF16)) for i in range(2)]
        qbuf = [ea(nc.sbuf_tensor(f"qbuf{i}", [128, S], BF16)) for i in range(2)]
        kbuf = [ea(nc.sbuf_tensor(f"kbuf{i}", [128, S], BF16)) for i in range(2)]
        vt2_sb = ea(nc.sbuf_tensor("vt2_sb", [128, NKC * VT_CW], BF16))
        sraw = ea(nc.sbuf_tensor("sraw", [128, NKC * QB], F32))
        pt_sb = ea(nc.sbuf_tensor("pt_sb", [128, NKC * QB], BF16))
        att_sb = ea(nc.sbuf_tensor("att_sb", [128, 8 * S], BF16))
        stile = ea(nc.sbuf_tensor("stile0", [128, D], F32))
        redtmp_a = ea(nc.sbuf_tensor("redtmp_a", [128, 4], F32))
        mq_col = ea(nc.sbuf_tensor("mq_col", [128, 4], F32))
        mq_row = ea(nc.sbuf_tensor("mq_row", [1, QB], BF16))
        mqb_a = ea(nc.sbuf_tensor("mqb_a", [128, QB], BF16))
        ones_row_bf = ea(nc.sbuf_tensor("ones_row_bf", [1, 128], BF16))
        srow_a = ea(nc.sbuf_tensor("srow_a", [1, QB], F32))
        rrow_a = srow_a
        rb_sb = ea(nc.sbuf_tensor("rb_sb", [64, QB], F32))

        SY.wait_ge(t_wt.h, wt_all)
        SY.wait_ge(t_xn.h, t_xn_all)
        SY.wait_ge(t_aux.h, tgt_wfm)
        SY.wait_ge(fence_s.h, p0_done["s"][1])
        SY.wait_ge(fence_v.h, p0_done["v"][1])
        for c in range(8):
            SY.dma_start(out=fct_w[:, c * D:(c + 1) * D], in_=scr_fcT[c]).then_inc(d_a.h, 16)
            d_a.inc(16)
        tgt_fct = d_a.n
        for c in range(8):
            SY.dma_start(out=wvt_w[:, c * D:(c + 1) * D], in_=scr_wvT[c]).then_inc(d_a.h, 16)
            d_a.inc(16)
        tgt_wvt = d_a.n

        if A_PAIR_PARTS & 1024:
            SY.dma_start(out=xn_a[0][:, 0:256], in_=scr_wvT[0][:, 0:256]).then_inc(d_xn.h, 16)
            d_xn.inc(16)
        VE.wait_ge(d_a.h, tgt_wvt)
        VE.memset(ones_row_bf[:], 1.0)
        for c in range(8):
            VE.tensor_tensor(out=wvt_w[:, c * D:(c + 1) * D], in0=wvt_w[:, c * D:(c + 1) * D],
                             in1=hm_fb[:], op=ALU.mult)
        wvt_ready = tok("vector")

        xn_slice_ctr = [0]
        xn_free_tgt = {}
        fc_store_tgts = []
        seq_av_pe = [None, None]

        def load_xn_slice(seq, t5):
            ctr = xn_slice_ctr[0]
            xn_slice_ctr[0] += 1
            buf = xn_a[ctr % 2]
            w = xn_free_tgt.get(ctr - 2)
            if w:
                wait_toks(SY, w)
            if d_xn.n >= 16 * 16:
                SY.wait_ge(d_xn.h, d_xn.n - 16 * 8)
            for c in range(8):
                SY.dma_start(out=buf[:, c * 512:(c + 1) * 512],
                             in_=scr_xn[c][:, seq * S + t5 * 512: seq * S + (t5 + 1) * 512]).then_inc(d_xn.h, 16)
                d_xn.inc(16)
            return ctr, buf, d_xn.n

        for s in range(NSEQ):
            if s > 0 and seq_av_pe[s - 1]:
                wait_toks(VE, seq_av_pe[s - 1])
            VE.memset(vt2_sb[:], 1.0)
            vt_ms_done = tok("vector")

            for t5 in range(4):
                ctr, xbuf, ld_tgt = load_xn_slice(s, t5)
                for t128 in range(4):
                    kc = t5 * 4 + t128
                    bA = (kc % 2) * 2
                    bB = bA + 1
                    if t128 == 0:
                        TE.wait_ge(d_xn.h, ld_tgt)
                        TE.wait_ge(fence_v.h, max(wvt_ready[1], vt_ms_done[1]))
                    for bx in (bA, bB):
                        if bank_free[bx] is not None and bank_free[bx][1]:
                            TE.wait_ge(bank_free[bx][0].h, bank_free[bx][1])
                    for c in range(8):
                        lhs = xbuf[:, c * 512 + t128 * 128: c * 512 + (t128 + 1) * 128]
                        mA = TE.matmul(out=pb[bA][0:128, 0:512], lhsT=lhs,
                                       rhs=wvt_w[:, c * D: c * D + 512], start=(c == 0), stop=(c == 7))
                        mB = TE.matmul(out=pb[bB][0:128, 0:512], lhsT=lhs,
                                       rhs=wvt_w[:, c * D + 512: (c + 1) * D],
                                       start=(c == 0), stop=(c == 7))
                        if c == 7:
                            mA.then_inc(br[bA].h, 1)
                            br[bA].inc()
                            mB.then_inc(br[bB].h, 1)
                            br[bB].inc()
                    for half, b in ((0, bA), (1, bB)):
                        def vev(v, kc=kc, half=half, b=b):
                            dst = vt2_sb[:, kc * VT_CW + half * 8 * 65: kc * VT_CW + (half + 1) * 8 * 65]
                            dst3 = dst.rearrange("p (h x) -> p h x", x=65)[:, :, 0:64]
                            src3 = pb[b][0:128, 0:512].rearrange("p (h x) -> p h x", x=64)
                            bias3 = bvhm_b[:, half * 512:(half + 1) * 512].rearrange("p (h x) -> p h x", x=64)
                            v.tensor_tensor(out=dst3, in0=src3, in1=bias3, op=ALU.add)
                        evac(b, "vector", vev)
                xn_free_tgt[ctr] = pe_tokens()

            pr_proj_pe = {}
            for pr in range(A_MAX_PR):
                pbuf = pr % 2
                w = pr_proj_pe.get(pr - 2)
                if w:
                    wait_toks(SY, w)
                if (A_PAIR_PARTS & 1) and not ((A_PAIR_PARTS & 8) and s > 0):
                    nld = 1 if (A_PAIR_PARTS & 64) else 8
                    dsem = d_xn if (A_PAIR_PARTS & 128) else d_a
                    deng = GP if (A_PAIR_PARTS & 512) else (SC if (A_PAIR_PARTS & 256) else SY)
                    if A_PAIR_PARTS & 4096:
                        wait_toks(SY, pe_tokens())
                    for c in range(nld):
                        if A_PAIR_PARTS & 8192:
                            src = scr_xn[c][:, 0:128]
                        elif A_PAIR_PARTS & 16:
                            src = scr_wvT[c][:, 0:256]
                        else:
                            src = scr_wqkT[pr, c]
                        dw = 128 if (A_PAIR_PARTS & 8192) else 256
                        dst = (xn_a[0][:, c * dw:(c + 1) * dw] if (A_PAIR_PARTS & 32)
                               else wqk_sb[pbuf][:, c * dw:(c + 1) * dw])
                        deng.dma_start(out=dst, in_=src).then_inc(dsem.h, 16)
                        dsem.inc(16)
                tgt_wqk = d_a.n

                for t5 in range(4):
                    if A_PAIR_PARTS & 2:
                        ctr, xbuf, ld_tgt = load_xn_slice(s, t5)
                    else:
                        ctr, xbuf, ld_tgt = 0, xn_a[0], 0
                    for hh in range(2 if ((A_PAIR_PARTS & 4) and not A_SKIP_PROJ) else 0):
                        h = 2 * pr + hh
                        b = hh
                        def qkbody(t, xbuf=xbuf, pbuf=pbuf, hh=hh, b=b):
                            for c in range(8):
                                last = t.matmul(out=pb[b][0:128, 0:512],
                                                lhsT=wqk_sb[pbuf][:, c * 256 + hh * 128: c * 256 + (hh + 1) * 128],
                                                rhs=xbuf[:, c * 512:(c + 1) * 512], start=(c == 0), stop=(c == 7))
                            return last
                        ew = [(d_a, tgt_wqk), (d_xn, ld_tgt)] if hh == 0 else []
                        pe_group(b, qkbody, extra_waits=ew)
                        def qkev(v, h=h, t5=t5, b=b, pbuf=pbuf):
                            base = (h % 2) * 64
                            if A_SIMPLE_QKEV:
                                v.tensor_copy(out=qbuf[pbuf][:, t5 * 512:(t5 + 1) * 512],
                                              in_=pb[b][0:128, 0:512])
                                return
                            v.tensor_scalar(out=qbuf[pbuf][base:base + 64, t5 * 512:(t5 + 1) * 512],
                                            in0=pb[b][0:64, 0:512], scalar1=hmcol[0:64, h:h + 1],
                                            scalar2=None, op0=ALU.mult)
                            v.tensor_scalar(out=qbuf[pbuf][base:base + 64, t5 * 512:(t5 + 1) * 512],
                                            in0=qbuf[pbuf][base:base + 64, t5 * 512:(t5 + 1) * 512],
                                            scalar1=bqkhm[0:64, h:h + 1], scalar2=None, op0=ALU.add)
                            v.tensor_scalar(out=kbuf[pbuf][base:base + 64, t5 * 512:(t5 + 1) * 512],
                                            in0=pb[b][64:128, 0:512], scalar1=hmcol[64:128, h:h + 1],
                                            scalar2=None, op0=ALU.mult)
                            v.tensor_scalar(out=kbuf[pbuf][base:base + 64, t5 * 512:(t5 + 1) * 512],
                                            in0=kbuf[pbuf][base:base + 64, t5 * 512:(t5 + 1) * 512],
                                            scalar1=bqkhm[64:128, h:h + 1], scalar2=None, op0=ALU.add)
                        evac(b, "vector", qkev)
                    xn_free_tgt[ctr] = pe_tokens()
                pr_proj_pe[pr] = pe_tokens()
                qk_done_v = tok("vector")

                units = [(h2, qb) for h2 in range(2) for qb in range(NQB)] if A_DO_UNITS else []
                prev = None
                for u in range(len(units) + 1):
                    if prev is not None:
                        up_h2, up_qb = prev["unit"]
                        hp = 2 * pr + up_h2
                        if prev.get("av_pe"):
                            wait_toks(SC, prev["av_pe"])
                        for kc in range(NKC):
                            SC.wait_ge(fence_v.h, prev["evac_tgts"][kc])
                            SC.activation(out=pt_sb[:, kc * QB:(kc + 1) * QB],
                                          in_=sraw[:, kc * QB:(kc + 1) * QB],
                                          func=AF.Exp, scale=0.125)
                        exp_t = tok("scalar")
                        def avbody(t, hp=hp):
                            for kc in range(NKC):
                                last = t.matmul(out=pb[4][0:65, 0:QB],
                                                lhsT=vt2_sb[:, kc * VT_CW + hp * 65: kc * VT_CW + (hp + 1) * 65],
                                                rhs=pt_sb[:, kc * QB:(kc + 1) * QB],
                                                start=(kc == 0), stop=(kc == NKC - 1))
                            return last
                        pe_group(4, avbody, extra_waits=[(fence_s, exp_t[1])])
                        av_rdy = br[4].n
                        av_pe = pe_tokens()
                        VE.wait_ge(br[4].h, av_rdy)
                        VE.tensor_copy(out=srow_a[:], in_=pb[4][64:65, 0:QB])
                        VE.reciprocal(out=rrow_a[:], in_=srow_a[:])
                        rcp_t = tok("vector")
                        pe_group(7, lambda t: t.matmul(out=pb[7][0:64, 0:QB], lhsT=ones_row[0:1, 0:64],
                                                       rhs=rrow_a[:], start=True, stop=True),
                                 extra_waits=[(fence_v, rcp_t[1])])
                        evac(7, "vector", lambda v: v.tensor_copy(out=rb_sb[:], in_=pb[7][0:64, 0:QB]))
                        def avep2(v, hp=hp, up_qb=up_qb):
                            dchunk = hp // 2
                            pbase = (hp % 2) * 64
                            v.tensor_tensor(
                                out=att_sb[pbase:pbase + 64, dchunk * S + up_qb * QB: dchunk * S + (up_qb + 1) * QB],
                                in0=pb[4][0:64, 0:QB], in1=rb_sb[:], op=ALU.mult)
                        evac(4, "vector", avep2)
                        prev_exp_t = exp_t
                        next_av_pe = av_pe
                    else:
                        prev_exp_t = None
                        next_av_pe = None
                    if u >= len(units):
                        prev = None
                        continue
                    h2, qb = units[u]
                    qkbase = h2 * 64
                    for j in range(4):
                        for kc5 in range(4):
                            b = 2 + (kc5 % 2)
                            def prebody(t, j=j, kc5=kc5, qb=qb, qkbase=qkbase, pbuf=pbuf, b=b,
                                        wq=(qk_done_v if (j == 0 and kc5 == 0 and u == 0) else None)):
                                if wq:
                                    t.wait_ge(fence_v.h, wq[1])
                                return t.matmul(out=pb[b][0:128, 0:512],
                                                lhsT=qbuf[pbuf][qkbase:qkbase + 64,
                                                                qb * QB + j * 128: qb * QB + (j + 1) * 128],
                                                rhs=kbuf[pbuf][qkbase:qkbase + 64, kc5 * 512:(kc5 + 1) * 512],
                                                start=True, stop=True)
                            pe_group(b, prebody)
                            evac(b, "vector", lambda v, kc5=kc5, b=b: v.tensor_reduce(
                                out=redtmp_a[:, kc5:kc5 + 1], in_=pb[b][0:128, 0:512],
                                axis=AX.X, op=ALU.max))
                        VE.tensor_reduce(out=mq_col[:, j:j + 1], in_=redtmp_a[:], axis=AX.X, op=ALU.max)
                    red_t = tok("vector")
                    for j in range(4):
                        pe_group(7, lambda t, j=j: t.transpose(out=pb[7][0:1, 0:128],
                                                               in_=mq_col[:, j:j + 1], identity=ident[:]),
                                 extra_waits=[(fence_v, red_t[1])] if j == 0 else [])
                        evac(7, "vector", lambda v, j=j: v.tensor_copy(
                            out=mq_row[0:1, j * 128:(j + 1) * 128], in_=pb[7][0:1, 0:128]))
                    mqr_t = tok("vector")
                    pe_group(7, lambda t: t.matmul(out=pb[7][0:128, 0:512], lhsT=ones_row_bf[:],
                                                   rhs=mq_row[:], start=True, stop=True),
                             extra_waits=[(fence_v, mqr_t[1])])
                    evac(7, "vector", lambda v: v.tensor_copy(out=mqb_a[:], in_=pb[7][0:128, 0:512]))
                    mqb_t = tok("vector")
                    dsub_tok_list = []
                    for kc in range(NKC):
                        b = 5 + (kc % 2)
                        if kc < 2:
                            bf0 = bank_free[b]
                            if bf0 is not None and bf0[1]:
                                TE.wait_ge(bf0[0].h, bf0[1])
                        else:
                            wtk = dsub_tok_list[kc - 2]
                            TE.wait_ge(wtk[0].h, wtk[1])
                        TE.matmul(out=pb[b][0:128, 0:QB],
                                  lhsT=kbuf[pbuf][qkbase:qkbase + 64, kc * 128:(kc + 1) * 128],
                                  rhs=qbuf[pbuf][qkbase:qkbase + 64, qb * QB:(qb + 1) * QB],
                                  start=True, stop=True).then_inc(br[b].h, 1)
                        sc_tgt = br[b].inc()
                        if kc == 0:
                            VE.wait_ge(fence_v.h, mqb_t[1])
                            if prev_exp_t:
                                VE.wait_ge(fence_s.h, prev_exp_t[1])
                        VE.wait_ge(br[b].h, sc_tgt)
                        VE.tensor_tensor(out=sraw[:, kc * QB:(kc + 1) * QB],
                                         in0=pb[b][0:128, 0:QB], in1=mqb_a[:],
                                         op=ALU.subtract)
                        tkn = tok("vector")
                        bank_free[b] = tkn
                        dsub_tok_list.append(tkn)
                    prev = {"unit": units[u], "evac_tgts": [t_[1] for t_ in dsub_tok_list],
                            "av_pe": next_av_pe}
            seq_av_pe[s] = pe_tokens()

            att_ready = tok("vector")
            for t128 in range(16 if A_DO_FC else 0):
                for e5 in range(2):
                    b = 5 if e5 == 0 else 6
                    def fcbody(t, t128=t128, e5=e5, b=b):
                        for c in range(8):
                            last = t.matmul(out=pb[b][0:128, 0:512],
                                            lhsT=att_sb[:, c * S + t128 * 128:c * S + (t128 + 1) * 128],
                                            rhs=fct_w[:, c * D + e5 * 512: c * D + (e5 + 1) * 512],
                                            start=(c == 0), stop=(c == 7))
                        return last
                    ew = [(fence_v, att_ready[1]), (d_a, tgt_fct)] if (t128 == 0 and e5 == 0) else []
                    pe_group(b, fcbody, extra_waits=ew)
                    ew2 = []
                    if len(fc_store_tgts) >= 1:
                        ew2 = [(t_s, fc_store_tgts[-1])]
                    def fcev(v, e5=e5, b=b):
                        v.tensor_tensor(out=stile[:, e5 * 512:(e5 + 1) * 512],
                                        in0=pb[b][0:128, 0:512],
                                        in1=fcb_b[:, e5 * 512:(e5 + 1) * 512], op=ALU.add)
                    evac(b, "vector", fcev, extra_waits=ew2)
                st_v = tok("vector")
                SY.wait_ge(fence_v.h, st_v[1])
                SY.dma_start(out=scr_s[s * S + t128 * 128: s * S + (t128 + 1) * 128, :],
                             in_=stile[:]).then_inc(t_s.h, 16)
                fc_store_tgts.append(t_s.inc(16))
        t_s_all = t_s.n
        a_done = dict(v=tok("vector"), s=tok("scalar"), pe=pe_tokens())
        a_ctx.close()

    if phase == "full":
        f_ctx = ExitStack()
        ef = f_ctx.enter_context
        xn_sb = ef(nc.sbuf_tensor("xn_sb", [128, 8 * TCH], BF16))
        wg_pan = [ef(nc.sbuf_tensor(f"wg_pan{i}", [128, 8 * 512], BF16)) for i in range(2)]
        wu_pan = [ef(nc.sbuf_tensor(f"wu_pan{i}", [128, 8 * 512], BF16)) for i in range(2)]
        wd2_sb = ef(nc.sbuf_tensor("wd2_sb", [128, 16 * 1024], BF16))
        G_sb = ef(nc.sbuf_tensor("G_sb", [128, 16 * TCH], BF16))
        ot_sb = ef(nc.sbuf_tensor("ot_sb", [128, 8 * D], F32))
        silu_sb = [ef(nc.sbuf_tensor(f"silu_sb{i}", [128, 512], BF16)) for i in range(2)]
        gtmp_sb = [ef(nc.sbuf_tensor(f"gtmp_sb{i}", [128, 512], BF16)) for i in range(2)]
        wb_sb = [ef(nc.sbuf_tensor(f"wb_sb{i}", [128, TCH], F32)) for i in range(2)]
        g_sb = [ef(nc.sbuf_tensor(f"g_sb{i}", [128, D], F32)) for i in range(2)]
        sx_sb = [ef(nc.sbuf_tensor(f"sx_sb{i}", [128, D], F32)) for i in range(2)]
        xx_sb = [ef(nc.sbuf_tensor(f"xx_sb{i}", [128, D], F32)) for i in range(2)]
        o_sb = [ef(nc.sbuf_tensor(f"o_sb{i}", [128, D], F32)) for i in range(2)]

        fb = a_done if emit_attention else p0_done

        pan_ctr = [0]
        pan_free = {}
        wb_user = {}
        prev_down_pe = None
        prev_tail_s = None
        tail_out_tgts = []
        last_umul = {}
        last_pool = {}
        last_tail_comb = {}

        def load_panels(sw, ip):
            ctr = pan_ctr[0]
            pan_ctr[0] += 1
            bufg = wg_pan[ctr % 2]
            bufu = wu_pan[ctr % 2]
            w = pan_free.get(ctr - 2)
            if w:
                wait_toks(SY, w)
            for c in range(8):
                SY.dma_start(out=bufg[:, c * 512:(c + 1) * 512],
                             in_=wg_bf[sw][c * 128:(c + 1) * 128, ip * 512:(ip + 1) * 512]).then_inc(d_w.h, 16)
            for c in range(8):
                SY.dma_start(out=bufu[:, c * 512:(c + 1) * 512],
                             in_=wu_bf[sw][c * 128:(c + 1) * 128, ip * 512:(ip + 1) * 512]).then_inc(d_w.h, 16)
            d_w.inc(16 * 16)
            return ctr, bufg, bufu, d_w.n

        for tc in range(NTCH):
            if tc == 0:
                SY.wait_ge(t_xn.h, t_xn_all)
                SY.wait_ge(t_aux.h, tgt_wfm)
                wait_toks(SY, fb["pe"])
                SY.wait_ge(fence_v.h, fb["v"][1])
                SY.wait_ge(fence_s.h, fb["s"][1])
                SY.wait_ge(d_cast.h, casts_all)
                if emit_attention:
                    SY.wait_ge(t_s.h, t_s_all)
            else:
                wait_toks(SY, pe_tokens())
            for c in range(8):
                SY.dma_start(out=xn_sb[:, c * TCH:(c + 1) * TCH],
                             in_=scr_xn[c][:, tc * TCH:(tc + 1) * TCH]).then_inc(d_xn.h, 16)
            d_xn.inc(16 * 8)
            xn_tgt = d_xn.n

            for sw in range(9):
                is_exp = sw > 0
                tgt_wb = None
                wbuf = None
                if is_exp:
                    wbuf = wb_sb[sw % 2]
                    w = wb_user.get(sw % 2)
                    if w:
                        SY.wait_ge(fence_g.h, w)
                    SY.dma_start(out=wbuf[:, 0:TCH],
                                 in_=scr_wfm[sw - 1, tc * TCH:(tc + 1) * TCH]
                                 .unsqueeze(0).partition_broadcast(128).squeeze(1)).then_inc(d_w.h, 16)
                    tgt_wb = d_w.inc(16)

                if prev_down_pe:
                    wait_toks(SY, prev_down_pe)
                for i in range(16):
                    SY.dma_start(out=wd2_sb[:, i * 1024:(i + 1) * 1024],
                                 in_=wd_bf[sw][i * 128:(i + 1) * 128, :]).then_inc(d_w.h, 16)
                d_w.inc(16 * 16)
                wd_tgt = d_w.n

                for ip in range(4):
                    ctr, bufg, bufu, pan_tgt = load_panels(sw, ip)
                    for il in range(4):
                        i128 = ip * 4 + il
                        gb = i128 % 2
                        ub = 2 + (i128 % 2)
                        for half in range(2):
                            def gbody(t, bufg=bufg, il=il, half=half, gb=gb):
                                for c in range(8):
                                    last = t.matmul(out=pb[gb][0:128, 0:512],
                                                    lhsT=bufg[:, c * 512 + il * 128: c * 512 + (il + 1) * 128],
                                                    rhs=xn_sb[:, c * TCH + half * 512: c * TCH + (half + 1) * 512],
                                                    start=(c == 0), stop=(c == 7))
                                return last
                            ew = [(d_w, pan_tgt), (d_xn, xn_tgt)] if (il == 0 and half == 0) else []
                            pe_group(gb, gbody, extra_waits=ew)
                            def ubody(t, bufu=bufu, il=il, half=half, ub=ub):
                                for c in range(8):
                                    last = t.matmul(out=pb[ub][0:128, 0:512],
                                                    lhsT=bufu[:, c * 512 + il * 128: c * 512 + (il + 1) * 128],
                                                    rhs=xn_sb[:, c * TCH + half * 512: c * TCH + (half + 1) * 512],
                                                    start=(c == 0), stop=(c == 7))
                                return last
                            pe_group(ub, ubody)
                            ew_s = []
                            if last_umul.get(i128 % 2):
                                ew_s = [(fence_v, last_umul[i128 % 2][1])]
                            evac(gb, "scalar", lambda sc, i128=i128, gb=gb: sc.activation(
                                out=silu_sb[i128 % 2][:], in_=pb[gb][0:128, 0:512], func=AF.Silu),
                                extra_waits=ew_s)
                            silu_t = tok("scalar")
                            Gdst = G_sb[:, i128 * TCH + half * 512: i128 * TCH + (half + 1) * 512]
                            dst = gtmp_sb[i128 % 2][:] if is_exp else Gdst
                            ew_u = [(fence_s, silu_t[1])]
                            if is_exp and last_pool.get(i128 % 2):
                                ew_u.append((fence_g, last_pool[i128 % 2][1]))
                            def umul(v, dst=dst, i128=i128, ub=ub):
                                v.tensor_tensor(out=dst, in0=pb[ub][0:128, 0:512],
                                                in1=silu_sb[i128 % 2][:], op=ALU.mult)
                            evac(ub, "vector", umul, extra_waits=ew_u)
                            um_t = tok("vector")
                            last_umul[i128 % 2] = um_t
                            if is_exp:
                                GP.wait_ge(fence_v.h, um_t[1])
                                GP.wait_ge(d_w.h, tgt_wb)
                                GP.tensor_tensor(out=Gdst, in0=gtmp_sb[i128 % 2][:],
                                                 in1=wbuf[:, half * 512:(half + 1) * 512],
                                                 op=ALU.mult)
                                last_pool[i128 % 2] = tok("gpsimd")
                    pan_free[ctr] = pe_tokens()
                if is_exp:
                    wb_user[sw % 2] = fence_g.n
                G_ready_p = fence_g.n
                G_ready_v = fence_v.n

                for t128 in range(8):
                    bA = 4 + (t128 % 2) * 2
                    bB = bA + 1
                    if t128 == 0:
                        if is_exp:
                            TE.wait_ge(fence_g.h, G_ready_p)
                        TE.wait_ge(fence_v.h, G_ready_v)
                        TE.wait_ge(d_w.h, wd_tgt)
                    for bx in (bA, bB):
                        if bank_free[bx] is not None and bank_free[bx][1]:
                            TE.wait_ge(bank_free[bx][0].h, bank_free[bx][1])
                    for i in range(16):
                        lhs = G_sb[:, i * TCH + t128 * 128: i * TCH + (t128 + 1) * 128]
                        mA = TE.matmul(out=pb[bA][0:128, 0:512], lhsT=lhs,
                                       rhs=wd2_sb[:, i * 1024: i * 1024 + 512], start=(i == 0), stop=(i == 15))
                        mB = TE.matmul(out=pb[bB][0:128, 0:512], lhsT=lhs,
                                       rhs=wd2_sb[:, i * 1024 + 512: (i + 1) * 1024],
                                       start=(i == 0), stop=(i == 15))
                        if i == 15:
                            mA.then_inc(br[bA].h, 1)
                            br[bA].inc()
                            mB.then_inc(br[bB].h, 1)
                            br[bB].inc()
                    for hh, b in ((0, bA), (1, bB)):
                        ew_d = []
                        if sw == 0 and prev_tail_s:
                            ew_d = [(fence_s, prev_tail_s[1])]
                        def dev(v, t128=t128, hh=hh, b=b, sw=sw):
                            dst = ot_sb[:, t128 * D + hh * 512: t128 * D + (hh + 1) * 512]
                            if sw == 0:
                                v.tensor_copy(out=dst, in_=pb[b][0:128, 0:512])
                            else:
                                v.tensor_tensor(out=dst, in0=dst, in1=pb[b][0:128, 0:512], op=ALU.add)
                        evac(b, "vector", dev, extra_waits=ew_d)
                prev_down_pe = pe_tokens()
            down_done_v = tok("vector")

            tail_silu_last = None
            for t128 in range(8):
                tok0 = tc * TCH + t128 * 128
                seq, ts0 = divmod(tok0, S)
                SY.dma_start(out=sx_sb[t128 % 2][:], in_=scr_s[tok0:tok0 + 128, :]).then_inc(d_s.h, 16)
                tgt_sx = d_s.inc(16)
                SY.dma_start(out=xx_sb[t128 % 2][:], in_=x_in[seq, ts0:ts0 + 128, :]).then_inc(d_x.h, 16)
                tgt_xx = d_x.inc(16)
                SC.wait_ge(fence_v.h, down_done_v[1])
                if last_tail_comb.get(t128 % 2):
                    SC.wait_ge(fence_v.h, last_tail_comb[t128 % 2][1])
                SC.activation(out=g_sb[t128 % 2][:], in_=ot_sb[:, t128 * D:(t128 + 1) * D], func=AF.Silu)
                ts_t = tok("scalar")
                tail_silu_last = ts_t
                VE.wait_ge(fence_s.h, ts_t[1])
                VE.wait_ge(d_s.h, tgt_sx)
                VE.wait_ge(d_x.h, tgt_xx)
                if len(tail_out_tgts) >= 2:
                    VE.wait_ge(t_out.h, tail_out_tgts[-2])
                VE.tensor_tensor(out=g_sb[t128 % 2][:], in0=g_sb[t128 % 2][:], in1=sx_sb[t128 % 2][:],
                                 op=ALU.mult)
                VE.tensor_tensor(out=o_sb[t128 % 2][:], in0=g_sb[t128 % 2][:], in1=xx_sb[t128 % 2][:],
                                 op=ALU.add)
                ocomb_t = tok("vector")
                last_tail_comb[t128 % 2] = ocomb_t
                SY.wait_ge(fence_v.h, ocomb_t[1])
                SY.dma_start(out=out_hbm[seq, ts0:ts0 + 128, :], in_=o_sb[t128 % 2][:]).then_inc(t_out.h, 16)
                tail_out_tgts.append(t_out.inc(16))
            prev_tail_s = tail_silu_last
        SY.wait_ge(t_out.h, t_out.n)
        f_ctx.close()

    if phase in ("p0", "p0a"):
        SY.wait_ge(t_xn.h, t_xn_all)
        SY.wait_ge(t_aux.h, tgt_wfm)
        if phase == "p0a":
            SY.wait_ge(t_s.h, t_s_all)
        for c in range(8):
            SY.dma_start(out=dbg["dbg_xn"][c], in_=scr_xn[c]).then_inc(t_out.h, 16)
            t_out.inc(16)
        SY.dma_start(out=dbg["dbg_wfm"][:], in_=scr_wfm[:]).then_inc(t_out.h, 16)
        t_out.inc(16)
        if phase == "p0a":
            SY.dma_start(out=dbg["dbg_s"][:], in_=scr_s[:]).then_inc(t_out.h, 16)
            t_out.inc(16)
        SY.wait_ge(t_out.h, t_out.n)


    for sem in (d_x, d_xn, d_w, d_a, d_s, d_sm, d_cast, t_xn, t_aux, t_s, t_wt, t_out):
        if sem.n:
            SY.wait_ge(sem.h, sem.n)

    with nc.Block() as blk:
        def replay(q, rec):
            for ent in rec.ops:
                fn = getattr(q, ent["m"])
                inst = fn(*ent["a"], **ent["k"])
                for (hh, amt) in ent["incs"]:
                    inst.then_inc(hh, amt)

        @blk.sync
        def _(q):
            replay(q, REC["sync"])

        @blk.tensor
        def _(q):
            replay(q, REC["tensor"])

        @blk.vector
        def _(q):
            replay(q, REC["vector"])

        @blk.scalar
        def _(q):
            replay(q, REC["scalar"])

        @blk.gpsimd
        def _(q):
            replay(q, REC["gpsimd"])

    ctx.close()
    return nc


def _in_maps(inputs):
    x = np.ascontiguousarray(inputs["L0_attn_x"])
    maps = []
    for c in range(NCORE):
        maps.append({
            "x": np.ascontiguousarray(x[2 * c:2 * c + 2]),
            "head_mask": np.ascontiguousarray(inputs["L0_attn_head_mask"]),
            "qkv_w": np.ascontiguousarray(inputs["L0_attn_qkv_w"]),
            "qkv_b": np.ascontiguousarray(inputs["L0_attn_qkv_b"]),
            "fc_w": np.ascontiguousarray(inputs["L0_attn_fc_w"]),
            "fc_b": np.ascontiguousarray(inputs["L0_attn_fc_b"]),
            "ln_scale": np.ascontiguousarray(inputs["L0_ffn_ln_scale"]),
            "ln_bias": np.ascontiguousarray(inputs["L0_ffn_ln_bias"]),
            "gate_w": np.ascontiguousarray(inputs["L0_ffn_gate_w"]),
            "Wg": np.ascontiguousarray(inputs["L0_ffn_Wg"]),
            "Wu": np.ascontiguousarray(inputs["L0_ffn_Wu"]),
            "Wd": np.ascontiguousarray(inputs["L0_ffn_Wd"]),
            "sWg": np.ascontiguousarray(inputs["L0_ffn_sWg"]),
            "sWu": np.ascontiguousarray(inputs["L0_ffn_sWu"]),
            "sWd": np.ascontiguousarray(inputs["L0_ffn_sWd"]),
        })
    return maps


def kernel(**inputs):
    nc = build_program(PHASE)
    res = run_bass_kernel_spmd(nc, _in_maps(inputs), core_ids=list(range(NCORE)))
    if PHASE == "full":
        return np.concatenate([res.results[c]["out"] for c in range(NCORE)], axis=0)
    return res
